# version 12
# baseline (speedup 1.0000x reference)
import numpy as np
import contextlib
import concourse.bass as bass
import concourse.mybir as mybir
from concourse.bass_utils import run_bass_kernel_spmd

F32 = mybir.dt.float32
BF16 = mybir.dt.bfloat16
AF = mybir.ActivationFunctionType
ALU = mybir.AluOpType

D = 1024
T = 2048
HID = 4096
NCD = 8
NCH = 32
NSUB = 4
EPS_LN = 1e-5
EPS_Z = 1e-8

ENGS = ("tensor", "vector", "scalar", "gpsimd", "sync")

A_LN2 = 0
A_KT = 16384
A_VT = 18432
A_QB = 20544
A_WKV = 26624


class Sem:
    def __init__(self, h):
        self.h = h
        self.n = 0


class Rec:
    def __init__(self):
        self.ops = {e: [] for e in ENGS}
        self.esem = {}
        self.dsem = {}
        self.waited = {e: {} for e in ENGS}

    def op(self, eng, meth, *args, **kw):
        s = self.esem[eng]
        s.n += 1
        v = s.n

        def fn(e, meth=meth, args=args, kw=kw, sh=s.h, v=v):
            getattr(e, meth)(*args, **kw).then_inc(sh, 1)

        self.ops[eng].append(fn)
        return (s, v)

    def dma(self, eng, ch, out, in_, serialize=True):
        s = self.dsem[ch]
        if serialize and s.n:
            self.wait(eng, (s, s.n))
        s.n += 16
        v = s.n

        def fn(e, out=out, in_=in_, sh=s.h):
            e.dma_start(out=out, in_=in_).then_inc(sh, 16)

        self.ops[eng].append(fn)
        return (s, v)

    def wait(self, eng, gate):
        if gate is None:
            return
        s, v = gate
        if v <= 0:
            return
        if self.waited[eng].get(id(s), 0) >= v:
            return
        assert v <= s.n, f"wait on future value {v} > {s.n} ({eng})"
        self.waited[eng][id(s)] = v

        def fn(e, sh=s.h, v=v):
            e.wait_ge(sh, v)

        self.ops[eng].append(fn)

    def waits(self, eng, gates):
        for g in gates:
            self.wait(eng, g)


class Pool:

    def __init__(self, rec, tiles):
        self.rec = rec
        self.items = [{"t": t, "free": None} for t in tiles]
        self.i = 0

    def take(self, eng):
        it = self.items[self.i % len(self.items)]
        self.i += 1
        self.rec.wait(eng, it["free"])
        return it


def build_nc(groups, debug=False):
    nc = bass.Bass("TRN2", target_bir_lowering=False)

    x_in = nc.dram_tensor("x", [D, T], F32, kind="ExternalInput")
    w1r_in = nc.dram_tensor("w1r", [NCH, 128, 1024], F32, kind="ExternalInput")
    w2r_in = nc.dram_tensor("w2r", [NCD, 128, 4096], F32, kind="ExternalInput")
    wqkv_in = nc.dram_tensor("wqkv", [D, 3 * D], F32, kind="ExternalInput")
    wproj_in = nc.dram_tensor("wproj", [D, D], F32, kind="ExternalInput")
    bkv_in = nc.dram_tensor("bkv", [1, 2 * D], F32, kind="ExternalInput")
    bqc_in = nc.dram_tensor("bqc", [128, NCD], F32, kind="ExternalInput")
    bprojc_in = nc.dram_tensor("bprojc", [128, NCD], F32, kind="ExternalInput")
    b1c_in = nc.dram_tensor("b1c", [128, NCH], F32, kind="ExternalInput")
    b2c_in = nc.dram_tensor("b2c", [128, NCD], F32, kind="ExternalInput")
    ind_in = nc.dram_tensor("ind", [16, 1024], BF16, kind="ExternalInput")
    y_out = nc.dram_tensor("y", [D, T], F32, kind="ExternalOutput")

    if debug:
        dbg_r1 = nc.dram_tensor("dbg_r1", [D, T], F32, kind="ExternalOutput")
        dbg_ln2 = nc.dram_tensor("dbg_ln2", [128, 16384], BF16,
                                 kind="ExternalOutput")
        dbg_kv = nc.dram_tensor("dbg_kv", [128, 520], F32,
                                kind="ExternalOutput")
        dbg_r2 = nc.dram_tensor("dbg_r2", [D, T], F32, kind="ExternalOutput")

    w1s_d = nc.dram_tensor("w1s_d", [NCH, 128, 1024], BF16)
    w2s_d = nc.dram_tensor("w2s_d", [NCD, 128, 4096], BF16)
    cc_in = nc.dram_tensor("cc_in", [128, 520], F32)
    cc_out = nc.dram_tensor("cc_out", [128, 520], F32)

    rec = Rec()
    ctx = contextlib.ExitStack()
    with ctx:
        x8 = ctx.enter_context(nc.sbuf_tensor("x8", [128, NCD * T], F32))
        lnA = ctx.enter_context(nc.sbuf_tensor("lnA", [128, NCD * 2048], BF16))
        A = ctx.enter_context(nc.sbuf_tensor("A", [128, 32768], BF16))
        C = ctx.enter_context(nc.sbuf_tensor("C", [128, 6144], BF16))
        xbf = ctx.enter_context(nc.sbuf_tensor("xbf", [128, 2 * 512], BF16))
        x2bf = ctx.enter_context(nc.sbuf_tensor("x2bf", [128, 2 * 512], BF16))
        tmp = [ctx.enter_context(nc.sbuf_tensor(f"tmp{i}", [128, 512], F32))
               for i in range(3)]
        rows_t = ctx.enter_context(nc.sbuf_tensor("rows_t", [128, 1536], F32))
        mrow = rows_t[:, 0:512]
        e2row = rows_t[:, 512:1024]
        s4row = rows_t[:, 1024:1536]
        rm4 = ctx.enter_context(nc.sbuf_tensor("rm4", [128, 4096], BF16))
        ones_sq = ctx.enter_context(nc.sbuf_tensor("ones_sq", [128, 128], BF16))
        sbf_t = ctx.enter_context(nc.sbuf_tensor("sbf_t", [16, 768], BF16))
        zbf = sbf_t[:, 0:512]
        ones_r = sbf_t[0:1, 640:768]
        kv_sb = ctx.enter_context(nc.sbuf_tensor("kv_sb", [128, 520], F32))
        kv_bf = ctx.enter_context(nc.sbuf_tensor("kv_bf", [128, 520], BF16))
        KSQ = ctx.enter_context(nc.sbuf_tensor("KSQ", [128, 128], BF16))
        ones_c = ctx.enter_context(nc.sbuf_tensor("ones_c", [128, 1], BF16))
        IND = ctx.enter_context(nc.sbuf_tensor("IND", [16, 1024], BF16))
        bkv_bf = ctx.enter_context(nc.sbuf_tensor("bkv_bf", [1, 2 * D], BF16))
        bias_t = ctx.enter_context(nc.sbuf_tensor("bias_t", [128, 128], F32))
        alpha1c = bias_t[:, 0:32]
        alpha2c = bias_t[:, 32:40]
        a2ch = bias_t[:, 40:48]
        b1c = bias_t[:, 48:80]
        b2c = bias_t[:, 80:88]
        b2ch = bias_t[:, 88:96]
        bqc = bias_t[:, 96:104]
        nbqc = bias_t[:, 104:112]
        bprojc = bias_t[:, 112:120]

        ps = [ctx.enter_context(nc.psum_tensor(f"ps{i}", [128, 512], F32))
              for i in range(4)]
        pk = [ctx.enter_context(nc.psum_tensor(f"pk{i}", [128, 512], F32))
              for i in range(2)]
        pz = ctx.enter_context(nc.psum_tensor("pz", [128, 512], F32))
        pw = ctx.enter_context(nc.psum_tensor("pw", [128, 512], F32))
        stat_banks = [pz, pw, pk[0], pk[1]]

        for e in ENGS:
            rec.esem[e] = Sem(ctx.enter_context(nc.semaphore(f"se_{e}")))
        for ch in ("cst", "xin", "w1s0", "w1s1", "w2s0", "w2s1", "yst",
                   "wpr0", "wpr1", "wb0", "wb1",
                   "bkvc", "wkvc", "wqc", "wpjc", "ccb", "dbg"):
            rec.dsem[ch] = Sem(ctx.enter_context(nc.semaphore(f"sd_{ch}")))
        cc_sem = Sem(ctx.enter_context(nc.semaphore("cc")))

        psum = Pool(rec, ps)
        tpool = Pool(rec, tmp)

        rec.dma("sync", "cst", out=b1c[:, :], in_=b1c_in[:, :], serialize=False)
        rec.dma("sync", "cst", out=b2c[:, :], in_=b2c_in[:, :], serialize=False)
        rec.dma("sync", "cst", out=bqc[:, :], in_=bqc_in[:, :], serialize=False)
        rec.dma("sync", "cst", out=bprojc[:, :], in_=bprojc_in[:, :],
                serialize=False)
        g_cst = rec.dma("sync", "cst", out=IND[:, :], in_=ind_in[:, :],
                        serialize=False)
        g_b2 = g_bq = g_ind = g_cst
        x_g = {}
        g_x = None
        for c in range(NCD):
            g_x = rec.dma("sync", "xin", out=x8[:, c * T:(c + 1) * T],
                          in_=x_in[c * 128:(c + 1) * 128, :], serialize=False)
        for c in range(NCD):
            for i in range(NSUB):
                x_g[(c, i)] = g_x

        g_bkv = rec.dma("gpsimd", "bkvc", out=bkv_bf[:, :], in_=bkv_in[:, :])
        rec.op("gpsimd", "memset", ones_c[:, :], 1.0)
        rec.op("gpsimd", "memset", ones_sq[:, :], 1.0)
        rec.op("gpsimd", "memset", rows_t[:, :], 0.0)
        g_rm4z = rec.op("gpsimd", "memset", rm4[:, :], 0.0)
        g_ones_r = rec.op("gpsimd", "memset", ones_r[:, :], 1.0)
        g_ones_c = g_ones_r
        g_ksq_ms = rec.op("gpsimd", "memset", KSQ[:, :], 0.0)

        rec.wait("vector", g_b2)
        rec.op("vector", "tensor_scalar", out=b2ch[:, :], in0=b2c[:, :],
               scalar1=0.5, scalar2=None, op0=ALU.mult)
        rec.wait("vector", g_bq)
        g_const_v = rec.op("vector", "tensor_scalar", out=nbqc[:, :], in0=bqc[:, :],
                           scalar1=-1.0, scalar2=None, op0=ALU.mult)

        def prep_weight(which, entry):
            if which == 1:
                n_strips, width, src, dst, psA = NCH, 1024, w1r_in, w1s_d, pz
                n_sl = 8
            else:
                n_strips, width, src, dst, psA = NCD, 4096, w2r_in, w2s_d, pw
                n_sl = 32
            slot_free = [list(entry), list(entry)]
            wb_wait = [list(entry), list(entry)]
            g_sig = g_ab = g_mm = None
            for j in range(n_strips):
                par = j % 2
                base = par * (3 * width)
                raw = A[:, base:base + width]
                sig = A[:, base + width:base + 2 * width]
                ab = A[:, base + 2 * width:base + 3 * width]
                rec.waits("gpsimd", slot_free[par])
                g_ld = rec.dma("gpsimd", f"wb{par}", out=raw, in_=src[j])

                rec.wait("scalar", g_ld)
                rec.waits("scalar", wb_wait[par])
                g_sig = rec.op("scalar", "sign", sig, raw)
                rec.wait("vector", g_ld)
                rec.wait("vector", g_sig)
                g_ab = rec.op("vector", "tensor_tensor", out=ab, in0=raw,
                              in1=sig, op=ALU.mult)

                rec.wait("sync", g_sig)
                wb_g = rec.dma("sync", f"wpr{par}", out=dst[j], in_=sig)
                wb_wait[par] = [wb_g]

                rec.wait("tensor", g_ab)
                rec.wait("tensor", g_ones_c)
                g_mm = None
                for sl in range(n_sl):
                    g_mm = rec.op("tensor", "matmul", psA[:, j:j + 1],
                                  ab[:, sl * 128:(sl + 1) * 128], ones_c[:, :],
                                  start=(sl == 0), stop=(sl == n_sl - 1))
                slot_free[par] = [g_mm, g_sig]
            return g_sig, g_ab, g_mm, wb_wait[0] + wb_wait[1]

        g1_sig, g1_ab, g_a1mm, g1_wb = prep_weight(1, [])
        rec.wait("scalar", g_a1mm)
        g_a1 = rec.op("scalar", "mul", alpha1c[:, :], pz[:, 0:NCH], 1.0 / 1024.0)
        ent2 = [g1_sig, g1_ab, g_a1mm] + g1_wb
        g2_sig, g2_ab, g_a2mm, g2_wb = prep_weight(2, ent2)
        rec.wait("scalar", g_a2mm)
        g_a2 = rec.op("scalar", "mul", alpha2c[:, :], pw[:, 0:NCD], 1.0 / 4096.0)
        rec.wait("vector", g_a2)
        g_a2h = rec.op("vector", "tensor_scalar", out=a2ch[:, :], in0=alpha2c[:, :],
                       scalar1=0.5, scalar2=None, op0=ALU.mult)

        wbfin = [(rec.dsem["wpr0"], rec.dsem["wpr0"].n),
                 (rec.dsem["wpr1"], rec.dsem["wpr1"].n)]
        g_w1wb = wbfin
        g_w2wb = wbfin

        lnst = {"pzfree": [g_a1]}

        xbf_free = [None, None]

        def ln_pass(dest_fn):
            rec.waits("tensor", lnst["pzfree"])
            g_smm = None
            for i in range(NSUB):
                st = stat_banks[i]
                for c in range(NCD):
                    cp = (i * NCD + c) % 2
                    xb = xbf[:, cp * 512:(cp + 1) * 512]
                    x2 = x2bf[:, cp * 512:(cp + 1) * 512]
                    xt = x8[:, c * T + i * 512: c * T + (i + 1) * 512]
                    rec.wait("vector", x_g[(c, i)])
                    rec.wait("scalar", x_g[(c, i)])
                    rec.wait("vector", xbf_free[cp])
                    rec.wait("scalar", xbf_free[cp])
                    g_xb = rec.op("vector", "tensor_copy", out=xb, in_=xt)
                    g_x2 = rec.op("scalar", "square", x2, xt)
                    rec.wait("tensor", g_xb)
                    rec.op("tensor", "matmul", st[0:1, :], ones_c[:, :], xb,
                           start=(c == 0), stop=(c == NCD - 1))
                    rec.wait("tensor", g_x2)
                    g_smm = rec.op("tensor", "matmul", st[32:33, :],
                                   ones_c[:, :], x2,
                                   start=(c == 0), stop=(c == NCD - 1))
                    xbf_free[cp] = g_smm

            rec.wait("scalar", g_smm)
            g_mr = g_e2 = None
            for i in range(NSUB):
                st = stat_banks[i]
                p = 32 * i
                g_mr = rec.op("scalar", "mul", mrow[p:p + 1, :], st[0:1, :],
                              1.0 / 1024.0)
                g_e2 = rec.op("scalar", "mul", e2row[p:p + 1, :], st[32:33, :],
                              1.0 / 1024.0)
            lnst["pzfree"] = [g_e2]
            rec.wait("scalar", g_mr)
            g_m2 = rec.op("scalar", "square", s4row[:, :], mrow[:, :])
            rec.wait("vector", g_m2)
            g_stt = rec.op("vector", "scalar_tensor_tensor", out=e2row[:, :],
                           in0=e2row[:, :], scalar=EPS_LN, in1=s4row[:, :],
                           op0=ALU.add, op1=ALU.subtract)
            rec.wait("vector", g_stt)
            g_var = rec.op("vector", "reciprocal", e2row[:, :], e2row[:, :])
            rec.wait("scalar", g_var)
            g_r = rec.op("scalar", "activation", s4row[:, :], e2row[:, :],
                         AF.Sqrt)
            rec.wait("scalar", g_r)
            rec.wait("scalar", g_rm4z)
            rec.wait("vector", g_r)
            rec.wait("vector", g_rm4z)
            g_rd = g_md = None
            for i in range(NSUB):
                p = 32 * i
                g_rd = rec.op("scalar", "copy",
                              rm4[p:p + 1, i * 512:(i + 1) * 512],
                              s4row[p:p + 1, :])
                g_md = rec.op("vector", "tensor_tensor",
                              out=rm4[p:p + 1, 2048 + i * 512:
                                      2048 + (i + 1) * 512],
                              in0=mrow[p:p + 1, :], in1=s4row[p:p + 1, :],
                              op=ALU.mult)

            g_lns = []
            for i in range(NSUB):
                br = psum.take("tensor")
                bm = psum.take("tensor")
                rec.wait("tensor", g_rd)
                rec.wait("tensor", g_md)
                rec.op("tensor", "matmul", br["t"][:, :], ones_sq[:, :],
                       rm4[:, i * 512:(i + 1) * 512], start=True, stop=True)
                g_bc = rec.op("tensor", "matmul", bm["t"][:, :], ones_sq[:, :],
                              rm4[:, 2048 + i * 512: 2048 + (i + 1) * 512],
                              start=True, stop=True)

                rec.wait("vector", g_bc)
                g_n = None
                for c0 in range(0, NCD, 3):
                    cs = list(range(c0, min(c0 + 3, NCD)))
                    tts = []
                    g_tt = None
                    for c in cs:
                        xt = x8[:, c * T + i * 512: c * T + (i + 1) * 512]
                        tt = tpool.take("vector")
                        g_tt = rec.op("vector", "tensor_tensor",
                                      out=tt["t"][:, :], in0=xt,
                                      in1=br["t"][:, :], op=ALU.mult)
                        tts.append(tt)
                    rec.wait("vector", g_tt)
                    for c in cs:
                        tt = tts[c - c0]
                        g_n = rec.op("vector", "tensor_tensor",
                                     out=dest_fn(c, i), in0=tt["t"][:, :],
                                     in1=bm["t"][:, :], op=ALU.subtract)
                        tt["free"] = g_n
                br["free"] = g_n
                bm["free"] = g_n
                g_lns.append(g_n)
            return g_lns

        wslot1 = [None, None]
        wslot2 = [None, None]
        h_entry_gates = list(wbfin) + [g2_sig]

        def ffn_tau(tau, g_ln):
            g_h = None
            for j in range(NCH):
                par = j % 2
                eng = "sync" if par == 0 else "scalar"
                st = C[:, par * 1024:(par + 1) * 1024]
                rec.wait(eng, wslot1[par])
                rec.waits(eng, g_w1wb)
                g_st = rec.dma(eng, f"w1s{par}", out=st, in_=w1s_d[j])

                banks = [psum.take("tensor"), psum.take("tensor")]
                rec.wait("tensor", g_st)
                rec.wait("tensor", g_ln)
                g_mm = None
                for c in range(NCD):
                    for s in range(2):
                        g_mm = rec.op(
                            "tensor", "matmul", banks[s]["t"][:, :],
                            st[:, c * 128:(c + 1) * 128],
                            lnA[:, c * 2048 + tau * 1024 + s * 512:
                                c * 2048 + tau * 1024 + (s + 1) * 512],
                            start=(c == 0), stop=(c == NCD - 1))
                wslot1[par] = g_mm
                rec.wait("scalar", g_mm)
                rec.waits("scalar", h_entry_gates)
                rec.wait("scalar", g_cst)
                rec.wait("scalar", g_a1)
                for s in range(2):
                    g_h = rec.op("scalar", "activation",
                                 A[:, j * 1024 + s * 512: j * 1024 + (s + 1) * 512],
                                 banks[s]["t"][:, :], AF.Gelu,
                                 bias=b1c[:, j:j + 1], scale=alpha1c[:, j:j + 1])
                    banks[s]["free"] = g_h

            x8g = None
            for g in range(NCD):
                banks = [psum.take("tensor"), psum.take("tensor")]
                g_mm = None
                for half in range(2):
                    par = half
                    eng = "sync" if par == 0 else "scalar"
                    st = C[:, 2048 + par * 2048: 2048 + (par + 1) * 2048]
                    rec.wait(eng, wslot2[par])
                    rec.waits(eng, g_w2wb)
                    g_st = rec.dma(eng, f"w2s{par}", out=st,
                                   in_=w2s_d[g][:, half * 2048:
                                                (half + 1) * 2048])
                    rec.wait("tensor", g_st)
                    rec.wait("tensor", g_h)
                    for kk in range(16):
                        k = half * 16 + kk
                        for s in range(2):
                            g_mm = rec.op(
                                "tensor", "matmul", banks[s]["t"][:, :],
                                st[:, kk * 128:(kk + 1) * 128],
                                A[:, k * 1024 + s * 512:
                                  k * 1024 + (s + 1) * 512],
                                start=(k == 0), stop=(k == NCH - 1))
                    wslot2[par] = g_mm
                rec.wait("scalar", g_mm)
                rec.wait("scalar", g_a2h)
                for s in range(2):
                    tt = tpool.take("scalar")
                    g_ev = rec.op("scalar", "activation", tt["t"][:, :],
                                  banks[s]["t"][:, :], AF.Identity,
                                  bias=b2ch[:, g:g + 1], scale=a2ch[:, g:g + 1])
                    banks[s]["free"] = g_ev
                    i = tau * 2 + s
                    xt = x8[:, g * T + i * 512: g * T + (i + 1) * 512]
                    rec.wait("vector", g_ev)
                    x8g = rec.op("vector", "tensor_tensor", out=xt, in0=xt,
                                 in1=tt["t"][:, :], op=ALU.add)
                    tt["free"] = x8g
                    x_g[(g, i)] = x8g
            return x8g

        def elu_evac(ps_ap, dest_ap, bcol, nbcol):
            ta = tpool.take("scalar")
            tb = tpool.take("scalar")
            tc = tpool.take("scalar")
            rec.op("scalar", "activation", tc["t"][:, :], ps_ap, AF.Relu,
                   bias=bcol, scale=1.0)
            g_a = rec.op("scalar", "activation", ta["t"][:, :], ps_ap, AF.Relu,
                         bias=nbcol, scale=-1.0)
            rec.wait("scalar", g_a)
            g_s = rec.op("scalar", "activation", tb["t"][:, :], ta["t"][:, :],
                         AF.Exp, bias=0.0, scale=-1.0)
            rec.wait("vector", g_s)
            g_v = rec.op("vector", "tensor_tensor", out=dest_ap,
                         in0=tc["t"][:, :], in1=tb["t"][:, :], op=ALU.add)
            ta["free"] = g_v
            tb["free"] = g_v
            tc["free"] = g_v
            return g_s, g_v

        g_lns = ln_pass(lambda c, i: lnA[:, c * 2048 + i * 512:
                                         c * 2048 + (i + 1) * 512])
        for tau in range(2):
            ffn_tau(tau, g_lns[tau * 2 + 1])

        if debug:
            for g in range(NCD):
                for i in range(NSUB):
                    rec.wait("sync", x_g[(g, i)])
                dg = rec.dma("sync", "dbg", out=dbg_r1[g * 128:(g + 1) * 128, :],
                             in_=x8[:, g * T:(g + 1) * T])
            rec.wait("vector", dg)

        rec.waits("vector", wbfin)
        g_ln2 = ln_pass(lambda c, i: A[:, c * 2048 + i * 512:
                                       c * 2048 + (i + 1) * 512])[-1]

        if debug:
            rec.wait("sync", g_ln2)
            dg = rec.dma("sync", "dbg", out=dbg_ln2[:, :], in_=A[:, 0:16384])
            rec.wait("vector", dg)

        rec.wait("gpsimd", wslot1[0])
        rec.wait("gpsimd", wslot1[1])
        rec.wait("gpsimd", wslot2[0])
        rec.wait("gpsimd", wslot2[1])
        vt_ones_ap = A[:, A_VT:A_VT + 2080].rearrange(
            "p (b h x) -> p b h x", b=2, x=65)[:, :, :, 64:65]
        g_ones = rec.op("gpsimd", "memset", vt_ones_ap, 1.0)

        wkv_ap = {}
        g_wkv = None
        for c in range(NCD):
            if c < 3:
                dst = C[:, c * 2048:(c + 1) * 2048]
            else:
                dst = lnA[:, (c - 3) * 2048:(c - 2) * 2048]
            wkv_ap[c] = dst
            rec.wait("gpsimd", wslot1[0])
            rec.wait("gpsimd", wslot1[1])
            rec.wait("gpsimd", wslot2[0])
            rec.wait("gpsimd", wslot2[1])
            g_wkv = rec.dma("gpsimd", "wkvc", out=dst,
                            in_=wqkv_in[c * 128:(c + 1) * 128, D:3 * D],
                            serialize=False)

        kvbanks = [pz, pw, pk[0], pk[1]]
        kt_free = [None, None]
        vt_free = [None, None]
        g_kvmm = None
        for i in range(16):
            par = i % 2
            ktb = A[:, A_KT + par * 1024: A_KT + (par + 1) * 1024]
            vtb = A[:, A_VT + par * 1040: A_VT + (par + 1) * 1040]
            banks = [psum.take("tensor") for _ in range(4)]
            rec.wait("tensor", g_wkv)
            rec.wait("tensor", g_ln2)
            rec.wait("tensor", g_bkv)
            for c in range(NCD):
                lnslice = A[:, c * 2048 + i * 128: c * 2048 + (i + 1) * 128]
                for ft in range(4):
                    rec.op("tensor", "matmul", banks[ft]["t"][:, :],
                           lnslice, wkv_ap[c][:, ft * 512:(ft + 1) * 512],
                           start=(c == 0), stop=False)
            g_prod = None
            for ft in range(4):
                g_prod = rec.op("tensor", "matmul", banks[ft]["t"][:, :],
                                ones_r[:, :], bkv_bf[0:1, ft * 512:(ft + 1) * 512],
                                start=False, stop=True)

            rec.wait("scalar", g_prod)
            rec.wait("scalar", kt_free[par])
            g_k = None
            for ft in range(2):
                _, g_k = elu_evac(banks[ft]["t"][:, :],
                                  ktb[:, ft * 512:(ft + 1) * 512], 0.0, 0.0)
                banks[ft]["free"] = g_k
            rec.wait("vector", g_prod)
            rec.wait("vector", vt_free[par])
            rec.wait("vector", g_ones)
            g_v = None
            for ft in range(2, 4):
                src = banks[ft]["t"][:, :].rearrange("p (h e) -> p h e", e=64)
                dstv = vtb.rearrange("p (h x) -> p h x", x=65)[
                    :, (ft - 2) * 8:(ft - 1) * 8, 0:64]
                g_v = rec.op("vector", "tensor_copy", out=dstv, in_=src)
                banks[ft]["free"] = g_v

            rec.wait("tensor", g_k)
            rec.wait("tensor", g_v)
            if i == 0:
                rec.waits("tensor", lnst["pzfree"])
                rec.wait("tensor", g_a2)
            for h in range(16):
                bank = kvbanks[(h % 2) * 2 + h // 8]
                s = (h % 8) // 2
                out_ap = bank[0:64, s * 65:(s + 1) * 65]
                g_kvmm = rec.op("tensor", "matmul", out_ap,
                                ktb[:, h * 64:(h + 1) * 64],
                                vtb[:, h * 65:(h + 1) * 65],
                                start=(i == 0 and s == 0),
                                stop=(i == 15 and s == 3))
            kt_free[par] = g_kvmm
            vt_free[par] = g_kvmm

        rec.wait("vector", g_kvmm)
        rec.wait("scalar", g_kvmm)
        rec.op("vector", "tensor_copy", out=kv_sb[0:64, 0:260],
               in_=kvbanks[0][0:64, 0:260])
        g_ev_v = rec.op("vector", "tensor_copy", out=kv_sb[0:64, 260:520],
                        in_=kvbanks[1][0:64, 0:260])
        rec.op("scalar", "copy", kv_sb[64:128, 0:260], kvbanks[2][0:64, 0:260])
        g_ev_s = rec.op("scalar", "copy", kv_sb[64:128, 260:520],
                        kvbanks[3][0:64, 0:260])
        rec.wait("gpsimd", g_ev_v)
        rec.wait("gpsimd", g_ev_s)
        g_ccin = rec.dma("gpsimd", "ccb", out=cc_in[:, :], in_=kv_sb[:, :])
        lnst["pzfree"] = [g_ccin]

        def wq_ap(c):
            if c < 6:
                return C[:, c * 1024:(c + 1) * 1024]
            return lnA[:, (c - 6) * 1024:(c - 5) * 1024]

        rec.wait("gpsimd", g_kvmm)
        g_wq = None
        for c in range(NCD):
            g_wq = rec.dma("gpsimd", "wqc", out=wq_ap(c),
                           in_=wqkv_in[c * 128:(c + 1) * 128, 0:D],
                           serialize=False)

        rec.wait("gpsimd", g_ccin)
        cc_sem.n += 1

        def cc_fn(e, v=cc_sem.n, sh=cc_sem.h, groups=groups):
            e.collective_compute(
                "AllReduce", ALU.add, replica_groups=groups,
                ins=[cc_in.ap()], outs=[cc_out.ap()]).then_inc(sh, 1)
            e.wait_ge(sh, v)

        rec.ops["gpsimd"].append(cc_fn)
        g_kvback = rec.dma("gpsimd", "ccb", out=kv_sb[:, :], in_=cc_out[:, :])

        if debug:
            rec.wait("sync", g_kvback)
            dg = rec.dma("sync", "dbg", out=dbg_kv[:, :], in_=kv_sb[:, :])
            rec.wait("vector", dg)

        rec.wait("vector", g_kvback)
        g_kvbf = rec.op("vector", "tensor_copy", out=kv_bf[:, :], in_=kv_sb[:, :])
        rec.wait("vector", g_kvbf)
        rec.wait("vector", g_ksq_ms)
        g_ksq = None
        for h in range(16):
            half, idx = h // 8, h % 8
            cc2, hpar = idx // 2, idx % 2
            col = half * 260 + cc2 * 65 + 64
            pr = slice(hpar * 64, (hpar + 1) * 64)
            dcol = (h // 2) * 16 + h
            g_ksq = rec.op("vector", "tensor_copy", out=KSQ[pr, dcol:dcol + 1],
                           in_=kv_bf[pr, col:col + 1])

        QBASE = {0: A_QB, 1: 16384}
        pzq_free = {0: None, 1: None}
        q_done = {}
        qmm_last = [None]
        attn_g = {}

        def q_block(i):
            base = QBASE[i % 2]
            g_q = None
            for qc in range(NCD):
                bq = psum.take("tensor")
                rec.wait("tensor", g_wq)
                g_mm = None
                for c in range(NCD):
                    g_mm = rec.op(
                        "tensor", "matmul", bq["t"][:, :],
                        wq_ap(c)[:, qc * 128:(qc + 1) * 128],
                        A[:, c * 2048 + i * 512: c * 2048 + (i + 1) * 512],
                        start=(c == 0), stop=(c == NCD - 1))
                rec.wait("scalar", g_mm)
                rec.wait("scalar", g_cst)
                rec.wait("scalar", g_const_v)
                if qc == 0:
                    rec.wait("scalar", pzq_free[i % 2])
                _, g_q = elu_evac(bq["t"][:, :],
                                  A[:, base + qc * 512: base + (qc + 1) * 512],
                                  bqc[:, qc:qc + 1], nbqc[:, qc:qc + 1])
                bq["free"] = g_q
            q_done[i] = g_q
            qmm_last[0] = (rec.esem["tensor"], rec.esem["tensor"].n)

        def zout_block(i):
            base = QBASE[i % 2]
            rec.wait("tensor", q_done[i])
            rec.wait("tensor", g_ksq)
            rec.waits("tensor", lnst["pzfree"])
            g_z = None
            for qc in range(NCD):
                g_z = rec.op("tensor", "matmul", pz[0:16, :],
                             KSQ[:, qc * 16:(qc + 1) * 16],
                             A[:, base + qc * 512: base + (qc + 1) * 512],
                             start=(qc == 0), stop=(qc == NCD - 1))
            rec.wait("vector", g_z)
            g_ze = rec.op("vector", "tensor_scalar", out=pz[0:16, :],
                          in0=pz[0:16, :], scalar1=EPS_Z, scalar2=None,
                          op0=ALU.add)
            rec.wait("vector", g_ze)
            g_zr = rec.op("vector", "reciprocal", pz[0:16, :], pz[0:16, :])
            rec.wait("vector", g_zr)
            g_zbf = rec.op("vector", "tensor_copy", out=zbf[:, :],
                           in_=pz[0:16, :])
            lnst["pzfree"] = [g_zbf, g_ccin]

            g_out = None
            for qc in range(NCD):
                bz = psum.take("tensor")
                bo = psum.take("tensor")
                rec.wait("tensor", g_zbf)
                rec.wait("tensor", g_ind)
                rec.op("tensor", "matmul", bz["t"][:, :],
                       IND[:, qc * 128:(qc + 1) * 128], zbf[:, :],
                       start=True, stop=True)
                h0, h1 = 2 * qc, 2 * qc + 1
                for hh, pr in ((h0, slice(0, 64)), (h1, slice(64, 128))):
                    half, idx = hh // 8, hh % 8
                    cc2 = idx // 2
                    col = half * 260 + cc2 * 65
                    rec.wait("tensor", g_kvbf)
                    g_mm = rec.op("tensor", "matmul", bo["t"][pr, :],
                                  kv_bf[pr, col:col + 64],
                                  A[pr, base + qc * 512: base + (qc + 1) * 512],
                                  start=True, stop=True)
                tt = tpool.take("scalar")
                rec.wait("scalar", g_mm)
                g_zc = rec.op("scalar", "copy", tt["t"][:, :], bz["t"][:, :])
                bz["free"] = g_zc
                rec.wait("vector", g_zc)
                g_out = rec.op("vector", "tensor_tensor",
                               out=A[:, qc * 2048 + i * 512:
                                     qc * 2048 + (i + 1) * 512],
                               in0=bo["t"][:, :], in1=tt["t"][:, :], op=ALU.mult)
                bo["free"] = g_out
                tt["free"] = g_out
                attn_g[(qc, i)] = g_out
            pzq_free[i % 2] = g_out

        q_block(0)
        q_block(1)
        for i in range(NSUB):
            zout_block(i)
            if i + 2 < NSUB:
                q_block(i + 2)

        last_q_mm = qmm_last[0]
        g_wp = None
        for c in range(NCD):
            rec.wait("gpsimd", last_q_mm)
            g_wp = rec.dma("gpsimd", "wpjc", out=wq_ap(c),
                           in_=wproj_in[c * 128:(c + 1) * 128, :],
                           serialize=False)

        for tau in range(2):
            for g in range(NCD):
                banks = [psum.take("tensor"), psum.take("tensor")]
                rec.wait("tensor", g_wp)
                for c in range(NCD):
                    for s in range(2):
                        i = tau * 2 + s
                        rec.wait("tensor", attn_g[(c, i)])
                        g_mm = rec.op(
                            "tensor", "matmul", banks[s]["t"][:, :],
                            wq_ap(c)[:, g * 128:(g + 1) * 128],
                            A[:, c * 2048 + i * 512: c * 2048 + (i + 1) * 512],
                            start=(c == 0), stop=(c == NCD - 1))
                rec.wait("scalar", g_mm)
                rec.wait("scalar", g_cst)
                for s in range(2):
                    i = tau * 2 + s
                    tt = tpool.take("scalar")
                    g_ev = rec.op("scalar", "activation", tt["t"][:, :],
                                  banks[s]["t"][:, :], AF.Identity,
                                  bias=bprojc[:, g:g + 1], scale=1.0)
                    banks[s]["free"] = g_ev
                    xt = x8[:, g * T + i * 512: g * T + (i + 1) * 512]
                    rec.wait("vector", g_ev)
                    x8g = rec.op("vector", "tensor_tensor", out=xt, in0=xt,
                                 in1=tt["t"][:, :], op=ALU.add)
                    tt["free"] = x8g
                    x_g[(g, i)] = x8g

        if debug:
            for g in range(NCD):
                for i in range(NSUB):
                    rec.wait("sync", x_g[(g, i)])
                dg = rec.dma("sync", "dbg", out=dbg_r2[g * 128:(g + 1) * 128, :],
                             in_=x8[:, g * T:(g + 1) * T])
            rec.wait("vector", dg)

        proj_done = (rec.esem["tensor"], rec.esem["tensor"].n)
        h_entry_gates.append(proj_done)
        wslot1[0] = proj_done
        wslot1[1] = proj_done
        wslot2[0] = proj_done
        wslot2[1] = proj_done

        g_lns = ln_pass(lambda c, i: lnA[:, c * 2048 + i * 512:
                                         c * 2048 + (i + 1) * 512])
        for tau in range(2):
            ffn_tau(tau, g_lns[tau * 2 + 1])

        for g in range(NCD):
            for i in range(NSUB):
                rec.wait("sync", x_g[(g, i)])
            rec.dma("sync", "yst", out=y_out[g * 128:(g + 1) * 128, :],
                    in_=x8[:, g * T:(g + 1) * T], serialize=False)
        fin = (rec.dsem["yst"], rec.dsem["yst"].n)

        def fin_fn(e, sh=fin[0].h, v=fin[1]):
            e.wait_ge(sh, v)

        rec.ops["sync"].append(fin_fn)
        rec.ops["gpsimd"].append(fin_fn)

        with nc.Block() as block:
            @block.tensor
            def _(e):
                for fn in rec.ops["tensor"]:
                    fn(e)

            @block.vector
            def _(e):
                for fn in rec.ops["vector"]:
                    fn(e)

            @block.scalar
            def _(e):
                for fn in rec.ops["scalar"]:
                    fn(e)

            @block.gpsimd
            def _(e):
                for fn in rec.ops["gpsimd"]:
                    fn(e)

            @block.sync
            def _(e):
                for fn in rec.ops["sync"]:
                    fn(e)

    return nc


def kernel(L0_attn_x, L0_attn_w_qkv, L0_attn_b_qkv, L0_attn_w_proj, L0_attn_b_proj,
           L0_ffn_w1, L0_ffn_b1, L0_ffn_w2, L0_ffn_b2):
    x = np.asarray(L0_attn_x, dtype=np.float32)
    wqkv = np.ascontiguousarray(np.asarray(L0_attn_w_qkv, dtype=np.float32))
    bqkv = np.asarray(L0_attn_b_qkv, dtype=np.float32)
    wproj = np.ascontiguousarray(np.asarray(L0_attn_w_proj, dtype=np.float32))
    bproj = np.asarray(L0_attn_b_proj, dtype=np.float32)
    w1 = np.asarray(L0_ffn_w1, dtype=np.float32)
    b1 = np.asarray(L0_ffn_b1, dtype=np.float32)
    w2 = np.asarray(L0_ffn_w2, dtype=np.float32)
    b2 = np.asarray(L0_ffn_b2, dtype=np.float32)

    w1r = np.ascontiguousarray(
        w1.T.reshape(NCD, 128, NCH, 128).transpose(2, 1, 0, 3).reshape(NCH, 128, 1024))
    w2r = np.ascontiguousarray(
        w2.T.reshape(NCH, 128, NCD, 128).transpose(2, 1, 0, 3).reshape(NCD, 128, 4096))
    b1c = np.ascontiguousarray(b1.reshape(NCH, 128).T)
    b2c = np.ascontiguousarray(b2.reshape(NCD, 128).T)
    bqc = np.ascontiguousarray(bqkv[0:D].reshape(NCD, 128).T)
    bkv = np.ascontiguousarray(bqkv[D:3 * D].reshape(1, 2 * D))
    bprojc = np.ascontiguousarray(bproj.reshape(NCD, 128).T)

    import ml_dtypes
    ind = np.zeros((16, 1024), dtype=ml_dtypes.bfloat16)
    for c in range(NCD):
        for par in range(2):
            ind[2 * c + par, c * 128 + par * 64: c * 128 + par * 64 + 64] = 1.0

    nc = build_nc([[0, 1], [2, 3], [4, 5], [6, 7]])
    in_maps = []
    for core in range(8):
        b, s = core // 2, core % 2
        xs = np.ascontiguousarray(x[b, s * T:(s + 1) * T, :].T)
        in_maps.append({
            "x": xs, "w1r": w1r, "w2r": w2r, "wqkv": wqkv, "wproj": wproj,
            "bkv": bkv, "bqc": bqc, "bprojc": bprojc, "b1c": b1c, "b2c": b2c,
            "ind": ind,
        })
    res = run_bass_kernel_spmd(nc, in_maps, core_ids=list(range(8)))
    out = np.empty((4, 4096, D), dtype=np.float32)
    for core in range(8):
        b, s = core // 2, core % 2
        out[b, s * T:(s + 1) * T, :] = np.asarray(res.results[core]["y"]).T
    return out


# version 13 (final)
# speedup vs baseline: 1.2093x; 1.2093x over previous
import numpy as np
import contextlib
import concourse.bass as bass
import concourse.mybir as mybir
from concourse.bass_utils import run_bass_kernel_spmd

F32 = mybir.dt.float32
BF16 = mybir.dt.bfloat16
AF = mybir.ActivationFunctionType
ALU = mybir.AluOpType

D = 1024
T = 2048
HID = 4096
NCD = 8
NCH = 32
NSUB = 4
EPS_LN = 1e-5
EPS_Z = 1e-8

ENGS = ("tensor", "vector", "scalar", "gpsimd", "sync")

A_LN2 = 0
A_KT = 16384
A_VT = 18432
A_QB = 20544
A_WKV = 26624


class Sem:
    def __init__(self, h):
        self.h = h
        self.n = 0


class Rec:
    def __init__(self):
        self.ops = {e: [] for e in ENGS}
        self.esem = {}
        self.dsem = {}
        self.waited = {e: {} for e in ENGS}

    def op(self, eng, meth, *args, **kw):
        s = self.esem[eng]
        s.n += 1
        v = s.n

        def fn(e, meth=meth, args=args, kw=kw, sh=s.h, v=v):
            getattr(e, meth)(*args, **kw).then_inc(sh, 1)

        self.ops[eng].append(fn)
        return (s, v)

    def dma(self, eng, ch, out, in_, serialize=True):
        s = self.dsem[ch]
        if serialize and s.n:
            self.wait(eng, (s, s.n))
        s.n += 16
        v = s.n

        def fn(e, out=out, in_=in_, sh=s.h):
            e.dma_start(out=out, in_=in_).then_inc(sh, 16)

        self.ops[eng].append(fn)
        return (s, v)

    def wait(self, eng, gate):
        if gate is None:
            return
        s, v = gate
        if v <= 0:
            return
        if self.waited[eng].get(id(s), 0) >= v:
            return
        assert v <= s.n, f"wait on future value {v} > {s.n} ({eng})"
        self.waited[eng][id(s)] = v

        def fn(e, sh=s.h, v=v):
            e.wait_ge(sh, v)

        self.ops[eng].append(fn)

    def waits(self, eng, gates):
        for g in gates:
            self.wait(eng, g)


class Pool:

    def __init__(self, rec, tiles):
        self.rec = rec
        self.items = [{"t": t, "free": None} for t in tiles]
        self.i = 0

    def take(self, eng):
        it = self.items[self.i % len(self.items)]
        self.i += 1
        self.rec.wait(eng, it["free"])
        return it


def build_nc(groups, debug=False):
    nc = bass.Bass("TRN2", target_bir_lowering=False)

    x_in = nc.dram_tensor("x", [D, T], F32, kind="ExternalInput")
    w1r_in = nc.dram_tensor("w1r", [NCH, 128, 1024], F32, kind="ExternalInput")
    w2r_in = nc.dram_tensor("w2r", [NCD, 128, 4096], F32, kind="ExternalInput")
    wqkv_in = nc.dram_tensor("wqkv", [D, 3 * D], F32, kind="ExternalInput")
    wproj_in = nc.dram_tensor("wproj", [D, D], F32, kind="ExternalInput")
    bkv_in = nc.dram_tensor("bkv", [1, 2 * D], F32, kind="ExternalInput")
    bqc_in = nc.dram_tensor("bqc", [128, NCD], F32, kind="ExternalInput")
    bprojc_in = nc.dram_tensor("bprojc", [128, NCD], F32, kind="ExternalInput")
    b1c_in = nc.dram_tensor("b1c", [128, NCH], F32, kind="ExternalInput")
    b2c_in = nc.dram_tensor("b2c", [128, NCD], F32, kind="ExternalInput")
    ind_in = nc.dram_tensor("ind", [16, 1024], BF16, kind="ExternalInput")
    y_out = nc.dram_tensor("y", [D, T], F32, kind="ExternalOutput")

    if debug:
        dbg_r1 = nc.dram_tensor("dbg_r1", [D, T], F32, kind="ExternalOutput")
        dbg_ln2 = nc.dram_tensor("dbg_ln2", [128, 16384], BF16,
                                 kind="ExternalOutput")
        dbg_kv = nc.dram_tensor("dbg_kv", [128, 520], F32,
                                kind="ExternalOutput")
        dbg_r2 = nc.dram_tensor("dbg_r2", [D, T], F32, kind="ExternalOutput")

    w1s_d = nc.dram_tensor("w1s_d", [NCH, 128, 1024], BF16)
    w2s_d = nc.dram_tensor("w2s_d", [NCD, 128, 4096], BF16)
    cc_in = nc.dram_tensor("cc_in", [128, 520], F32)
    cc_out = nc.dram_tensor("cc_out", [128, 520], F32)

    rec = Rec()
    ctx = contextlib.ExitStack()
    with ctx:
        x8 = ctx.enter_context(nc.sbuf_tensor("x8", [128, NCD * T], F32))
        lnA = ctx.enter_context(nc.sbuf_tensor("lnA", [128, NCD * 2048], BF16))
        A = ctx.enter_context(nc.sbuf_tensor("A", [128, 32768], BF16))
        C = ctx.enter_context(nc.sbuf_tensor("C", [128, 6144], BF16))
        xbf = ctx.enter_context(nc.sbuf_tensor("xbf", [128, 2 * 512], BF16))
        x2bf = ctx.enter_context(nc.sbuf_tensor("x2bf", [128, 2 * 512], BF16))
        tmp = [ctx.enter_context(nc.sbuf_tensor(f"tmp{i}", [128, 512], F32))
               for i in range(3)]
        rows_t = ctx.enter_context(nc.sbuf_tensor("rows_t", [128, 1536], F32))
        mrow = rows_t[:, 0:512]
        e2row = rows_t[:, 512:1024]
        s4row = rows_t[:, 1024:1536]
        rm4 = ctx.enter_context(nc.sbuf_tensor("rm4", [128, 4096], BF16))
        ones_sq = ctx.enter_context(nc.sbuf_tensor("ones_sq", [128, 128], BF16))
        sbf_t = ctx.enter_context(nc.sbuf_tensor("sbf_t", [16, 768], BF16))
        zbf = sbf_t[:, 0:512]
        ones_r = sbf_t[0:1, 640:768]
        kv_sb = ctx.enter_context(nc.sbuf_tensor("kv_sb", [128, 520], F32))
        kv_bf = ctx.enter_context(nc.sbuf_tensor("kv_bf", [128, 520], BF16))
        KSQ = ctx.enter_context(nc.sbuf_tensor("KSQ", [128, 128], BF16))
        ones_c = ctx.enter_context(nc.sbuf_tensor("ones_c", [128, 1], BF16))
        IND = ctx.enter_context(nc.sbuf_tensor("IND", [16, 1024], BF16))
        bkv_bf = ctx.enter_context(nc.sbuf_tensor("bkv_bf", [1, 2 * D], BF16))
        bias_t = ctx.enter_context(nc.sbuf_tensor("bias_t", [128, 128], F32))
        alpha1c = bias_t[:, 0:32]
        alpha2c = bias_t[:, 32:40]
        a2ch = bias_t[:, 40:48]
        b1c = bias_t[:, 48:80]
        b2c = bias_t[:, 80:88]
        b2ch = bias_t[:, 88:96]
        bqc = bias_t[:, 96:104]
        nbqc = bias_t[:, 104:112]
        bprojc = bias_t[:, 112:120]

        ps = [ctx.enter_context(nc.psum_tensor(f"ps{i}", [128, 512], F32))
              for i in range(4)]
        pk = [ctx.enter_context(nc.psum_tensor(f"pk{i}", [128, 512], F32))
              for i in range(2)]
        pz = ctx.enter_context(nc.psum_tensor("pz", [128, 512], F32))
        pw = ctx.enter_context(nc.psum_tensor("pw", [128, 512], F32))
        stat_banks = [pz, pw, pk[0], pk[1]]

        for e in ENGS:
            rec.esem[e] = Sem(ctx.enter_context(nc.semaphore(f"se_{e}")))
        for ch in ("cst", "xin", "w1s0", "w1s1", "w2s0", "w2s1", "yst",
                   "wpr0", "wpr1", "wb0", "wb1",
                   "bkvc", "wkvc", "wqc", "wpjc", "ccb", "dbg"):
            rec.dsem[ch] = Sem(ctx.enter_context(nc.semaphore(f"sd_{ch}")))
        cc_sem = Sem(ctx.enter_context(nc.semaphore("cc")))

        psum = Pool(rec, ps)
        tpool = Pool(rec, tmp)

        rec.dma("sync", "cst", out=b1c[:, :], in_=b1c_in[:, :], serialize=False)
        rec.dma("sync", "cst", out=b2c[:, :], in_=b2c_in[:, :], serialize=False)
        rec.dma("sync", "cst", out=bqc[:, :], in_=bqc_in[:, :], serialize=False)
        rec.dma("sync", "cst", out=bprojc[:, :], in_=bprojc_in[:, :],
                serialize=False)
        g_cst = rec.dma("sync", "cst", out=IND[:, :], in_=ind_in[:, :],
                        serialize=False)
        g_b2 = g_bq = g_ind = g_cst
        x_g = {}
        g_x = None
        for c in range(NCD):
            g_x = rec.dma("sync", "xin", out=x8[:, c * T:(c + 1) * T],
                          in_=x_in[c * 128:(c + 1) * 128, :], serialize=False)
        for c in range(NCD):
            for i in range(NSUB):
                x_g[(c, i)] = g_x

        g_bkv = rec.dma("gpsimd", "bkvc", out=bkv_bf[:, :], in_=bkv_in[:, :])
        rec.op("gpsimd", "memset", ones_c[:, :], 1.0)
        rec.op("gpsimd", "memset", ones_sq[:, :], 1.0)
        rec.op("gpsimd", "memset", rows_t[:, :], 0.0)
        g_rm4z = rec.op("gpsimd", "memset", rm4[:, :], 0.0)
        g_ones_r = rec.op("gpsimd", "memset", ones_r[:, :], 1.0)
        g_ones_c = g_ones_r
        g_ksq_ms = rec.op("gpsimd", "memset", KSQ[:, :], 0.0)

        rec.wait("vector", g_b2)
        rec.op("vector", "tensor_scalar", out=b2ch[:, :], in0=b2c[:, :],
               scalar1=0.5, scalar2=None, op0=ALU.mult)
        rec.wait("vector", g_bq)
        g_const_v = rec.op("vector", "tensor_scalar", out=nbqc[:, :], in0=bqc[:, :],
                           scalar1=-1.0, scalar2=None, op0=ALU.mult)

        def prep_weight(which, entry):
            if which == 1:
                n_strips, width, src, dst, psA = NCH, 1024, w1r_in, w1s_d, pz
                n_sl = 8
            else:
                n_strips, width, src, dst, psA = NCD, 4096, w2r_in, w2s_d, pw
                n_sl = 32
            slot_free = [list(entry), list(entry)]
            wb_wait = [list(entry), list(entry)]
            g_sig = g_ab = g_mm = None
            for j in range(n_strips):
                par = j % 2
                base = par * (3 * width)
                raw = A[:, base:base + width]
                sig = A[:, base + width:base + 2 * width]
                ab = A[:, base + 2 * width:base + 3 * width]
                rec.waits("gpsimd", slot_free[par])
                g_ld = rec.dma("gpsimd", f"wb{par}", out=raw, in_=src[j])

                rec.wait("scalar", g_ld)
                rec.waits("scalar", wb_wait[par])
                g_sig = rec.op("scalar", "sign", sig, raw)
                rec.wait("vector", g_ld)
                rec.wait("vector", g_sig)
                g_ab = rec.op("vector", "tensor_tensor", out=ab, in0=raw,
                              in1=sig, op=ALU.mult)

                rec.wait("sync", g_sig)
                wb_g = rec.dma("sync", f"wpr{par}", out=dst[j], in_=sig)
                wb_wait[par] = [wb_g]

                rec.wait("tensor", g_ab)
                rec.wait("tensor", g_ones_c)
                g_mm = None
                for sl in range(n_sl):
                    g_mm = rec.op("tensor", "matmul", psA[:, j:j + 1],
                                  ab[:, sl * 128:(sl + 1) * 128], ones_c[:, :],
                                  start=(sl == 0), stop=(sl == n_sl - 1))
                slot_free[par] = [g_mm, g_sig]
            return g_sig, g_ab, g_mm, wb_wait[0] + wb_wait[1]

        g1_sig, g1_ab, g_a1mm, g1_wb = prep_weight(1, [])
        rec.wait("scalar", g_a1mm)
        g_a1 = rec.op("scalar", "mul", alpha1c[:, :], pz[:, 0:NCH], 1.0 / 1024.0)
        ent2 = [g1_sig, g1_ab, g_a1mm] + g1_wb
        g2_sig, g2_ab, g_a2mm, g2_wb = prep_weight(2, ent2)
        rec.wait("scalar", g_a2mm)
        g_a2 = rec.op("scalar", "mul", alpha2c[:, :], pw[:, 0:NCD], 1.0 / 4096.0)
        rec.wait("vector", g_a2)
        g_a2h = rec.op("vector", "tensor_scalar", out=a2ch[:, :], in0=alpha2c[:, :],
                       scalar1=0.5, scalar2=None, op0=ALU.mult)

        wbfin = [(rec.dsem["wpr0"], rec.dsem["wpr0"].n),
                 (rec.dsem["wpr1"], rec.dsem["wpr1"].n)]
        g_w1wb = wbfin
        g_w2wb = wbfin

        lnst = {"pzfree": [g_a1]}

        xbf_free = [None, None]

        def ln_pass(dest_fn):
            rec.waits("tensor", lnst["pzfree"])
            g_smm = None
            for i in range(NSUB):
                st = stat_banks[i]
                for c in range(NCD):
                    cp = (i * NCD + c) % 2
                    xb = xbf[:, cp * 512:(cp + 1) * 512]
                    x2 = x2bf[:, cp * 512:(cp + 1) * 512]
                    xt = x8[:, c * T + i * 512: c * T + (i + 1) * 512]
                    rec.wait("vector", x_g[(c, i)])
                    rec.wait("scalar", x_g[(c, i)])
                    rec.wait("vector", xbf_free[cp])
                    rec.wait("scalar", xbf_free[cp])
                    g_xb = rec.op("vector", "tensor_copy", out=xb, in_=xt)
                    g_x2 = rec.op("scalar", "square", x2, xt)
                    rec.wait("tensor", g_xb)
                    rec.op("tensor", "matmul", st[0:1, :], ones_c[:, :], xb,
                           start=(c == 0), stop=(c == NCD - 1))
                    rec.wait("tensor", g_x2)
                    g_smm = rec.op("tensor", "matmul", st[32:33, :],
                                   ones_c[:, :], x2,
                                   start=(c == 0), stop=(c == NCD - 1))
                    xbf_free[cp] = g_smm

            rec.wait("scalar", g_smm)
            g_mr = g_e2 = None
            for i in range(NSUB):
                st = stat_banks[i]
                p = 32 * i
                g_mr = rec.op("scalar", "mul", mrow[p:p + 1, :], st[0:1, :],
                              1.0 / 1024.0)
                g_e2 = rec.op("scalar", "mul", e2row[p:p + 1, :], st[32:33, :],
                              1.0 / 1024.0)
            lnst["pzfree"] = [g_e2]
            rec.wait("scalar", g_mr)
            g_m2 = rec.op("scalar", "square", s4row[:, :], mrow[:, :])
            rec.wait("vector", g_m2)
            g_stt = rec.op("vector", "scalar_tensor_tensor", out=e2row[:, :],
                           in0=e2row[:, :], scalar=EPS_LN, in1=s4row[:, :],
                           op0=ALU.add, op1=ALU.subtract)
            rec.wait("vector", g_stt)
            g_var = rec.op("vector", "reciprocal", e2row[:, :], e2row[:, :])
            rec.wait("scalar", g_var)
            g_r = rec.op("scalar", "activation", s4row[:, :], e2row[:, :],
                         AF.Sqrt)
            rec.wait("scalar", g_r)
            rec.wait("scalar", g_rm4z)
            rec.wait("vector", g_r)
            rec.wait("vector", g_rm4z)
            g_rd = g_md = None
            for i in range(NSUB):
                p = 32 * i
                g_rd = rec.op("scalar", "copy",
                              rm4[p:p + 1, i * 512:(i + 1) * 512],
                              s4row[p:p + 1, :])
                g_md = rec.op("vector", "tensor_tensor",
                              out=rm4[p:p + 1, 2048 + i * 512:
                                      2048 + (i + 1) * 512],
                              in0=mrow[p:p + 1, :], in1=s4row[p:p + 1, :],
                              op=ALU.mult)

            g_lns = []
            for i in range(NSUB):
                br = psum.take("tensor")
                bm = psum.take("tensor")
                rec.wait("tensor", g_rd)
                rec.wait("tensor", g_md)
                rec.op("tensor", "matmul", br["t"][:, :], ones_sq[:, :],
                       rm4[:, i * 512:(i + 1) * 512], start=True, stop=True)
                g_bc = rec.op("tensor", "matmul", bm["t"][:, :], ones_sq[:, :],
                              rm4[:, 2048 + i * 512: 2048 + (i + 1) * 512],
                              start=True, stop=True)

                rec.wait("vector", g_bc)
                g_n = None
                for c0 in range(0, NCD, 3):
                    cs = list(range(c0, min(c0 + 3, NCD)))
                    tts = []
                    g_tt = None
                    for c in cs:
                        xt = x8[:, c * T + i * 512: c * T + (i + 1) * 512]
                        tt = tpool.take("vector")
                        g_tt = rec.op("vector", "tensor_tensor",
                                      out=tt["t"][:, :], in0=xt,
                                      in1=br["t"][:, :], op=ALU.mult)
                        tts.append(tt)
                    rec.wait("vector", g_tt)
                    for c in cs:
                        tt = tts[c - c0]
                        g_n = rec.op("vector", "tensor_tensor",
                                     out=dest_fn(c, i), in0=tt["t"][:, :],
                                     in1=bm["t"][:, :], op=ALU.subtract)
                        tt["free"] = g_n
                br["free"] = g_n
                bm["free"] = g_n
                g_lns.append(g_n)
            return g_lns

        wslot1 = [None, None]
        wslot2 = [None, None]
        h_entry_gates = list(wbfin) + [g2_sig]

        def ffn_tau(tau, g_ln):
            g_h = None
            for j in range(NCH):
                par = j % 2
                st = C[:, par * 1024:(par + 1) * 1024]
                rec.wait("sync", wslot1[par])
                rec.waits("sync", g_w1wb)
                g_st = rec.dma("sync", f"w1s{par}", out=st, in_=w1s_d[j])

                banks = [psum.take("tensor"), psum.take("tensor")]
                rec.wait("tensor", g_st)
                rec.wait("tensor", g_ln)
                g_mm = None
                for c in range(NCD):
                    for s in range(2):
                        g_mm = rec.op(
                            "tensor", "matmul", banks[s]["t"][:, :],
                            st[:, c * 128:(c + 1) * 128],
                            lnA[:, c * 2048 + tau * 1024 + s * 512:
                                c * 2048 + tau * 1024 + (s + 1) * 512],
                            start=(c == 0), stop=(c == NCD - 1))
                wslot1[par] = g_mm
                rec.wait("scalar", g_mm)
                rec.waits("scalar", h_entry_gates)
                rec.wait("scalar", g_cst)
                rec.wait("scalar", g_a1)
                for s in range(2):
                    g_h = rec.op("scalar", "activation",
                                 A[:, j * 1024 + s * 512: j * 1024 + (s + 1) * 512],
                                 banks[s]["t"][:, :], AF.Gelu,
                                 bias=b1c[:, j:j + 1], scale=alpha1c[:, j:j + 1])
                    banks[s]["free"] = g_h

            x8g = None
            for g in range(NCD):
                banks = [psum.take("tensor"), psum.take("tensor")]
                g_mm = None
                for half in range(2):
                    par = half
                    st = C[:, 2048 + par * 2048: 2048 + (par + 1) * 2048]
                    rec.wait("sync", wslot2[par])
                    rec.waits("sync", g_w2wb)
                    g_st = rec.dma("sync", f"w2s{par}", out=st,
                                   in_=w2s_d[g][:, half * 2048:
                                                (half + 1) * 2048])
                    rec.wait("tensor", g_st)
                    rec.wait("tensor", g_h)
                    for kk in range(16):
                        k = half * 16 + kk
                        for s in range(2):
                            g_mm = rec.op(
                                "tensor", "matmul", banks[s]["t"][:, :],
                                st[:, kk * 128:(kk + 1) * 128],
                                A[:, k * 1024 + s * 512:
                                  k * 1024 + (s + 1) * 512],
                                start=(k == 0), stop=(k == NCH - 1))
                    wslot2[par] = g_mm
                rec.wait("scalar", g_mm)
                rec.wait("scalar", g_a2h)
                for s in range(2):
                    tt = tpool.take("scalar")
                    g_ev = rec.op("scalar", "activation", tt["t"][:, :],
                                  banks[s]["t"][:, :], AF.Identity,
                                  bias=b2ch[:, g:g + 1], scale=a2ch[:, g:g + 1])
                    banks[s]["free"] = g_ev
                    i = tau * 2 + s
                    xt = x8[:, g * T + i * 512: g * T + (i + 1) * 512]
                    rec.wait("vector", g_ev)
                    x8g = rec.op("vector", "tensor_tensor", out=xt, in0=xt,
                                 in1=tt["t"][:, :], op=ALU.add)
                    tt["free"] = x8g
                    x_g[(g, i)] = x8g
            return x8g

        def elu_evac(ps_ap, dest_ap, bcol, nbcol):
            ta = tpool.take("scalar")
            tb = tpool.take("scalar")
            tc = tpool.take("scalar")
            rec.op("scalar", "activation", tc["t"][:, :], ps_ap, AF.Relu,
                   bias=bcol, scale=1.0)
            g_a = rec.op("scalar", "activation", ta["t"][:, :], ps_ap, AF.Relu,
                         bias=nbcol, scale=-1.0)
            rec.wait("scalar", g_a)
            g_s = rec.op("scalar", "activation", tb["t"][:, :], ta["t"][:, :],
                         AF.Exp, bias=0.0, scale=-1.0)
            rec.wait("vector", g_s)
            g_v = rec.op("vector", "tensor_tensor", out=dest_ap,
                         in0=tc["t"][:, :], in1=tb["t"][:, :], op=ALU.add)
            ta["free"] = g_v
            tb["free"] = g_v
            tc["free"] = g_v
            return g_s, g_v

        g_lns = ln_pass(lambda c, i: lnA[:, c * 2048 + i * 512:
                                         c * 2048 + (i + 1) * 512])
        for tau in range(2):
            ffn_tau(tau, g_lns[tau * 2 + 1])

        if debug:
            for g in range(NCD):
                for i in range(NSUB):
                    rec.wait("sync", x_g[(g, i)])
                dg = rec.dma("sync", "dbg", out=dbg_r1[g * 128:(g + 1) * 128, :],
                             in_=x8[:, g * T:(g + 1) * T])
            rec.wait("vector", dg)

        rec.waits("vector", wbfin)
        g_ln2 = ln_pass(lambda c, i: A[:, c * 2048 + i * 512:
                                       c * 2048 + (i + 1) * 512])[-1]

        if debug:
            rec.wait("sync", g_ln2)
            dg = rec.dma("sync", "dbg", out=dbg_ln2[:, :], in_=A[:, 0:16384])
            rec.wait("vector", dg)

        rec.wait("gpsimd", wslot1[0])
        rec.wait("gpsimd", wslot1[1])
        rec.wait("gpsimd", wslot2[0])
        rec.wait("gpsimd", wslot2[1])
        vt_ones_ap = A[:, A_VT:A_VT + 2080].rearrange(
            "p (b h x) -> p b h x", b=2, x=65)[:, :, :, 64:65]
        g_ones = rec.op("gpsimd", "memset", vt_ones_ap, 1.0)

        wkv_ap = {}
        g_wkv = None
        for c in range(NCD):
            if c < 3:
                dst = C[:, c * 2048:(c + 1) * 2048]
            else:
                dst = lnA[:, (c - 3) * 2048:(c - 2) * 2048]
            wkv_ap[c] = dst
            rec.wait("gpsimd", wslot1[0])
            rec.wait("gpsimd", wslot1[1])
            rec.wait("gpsimd", wslot2[0])
            rec.wait("gpsimd", wslot2[1])
            g_wkv = rec.dma("gpsimd", "wkvc", out=dst,
                            in_=wqkv_in[c * 128:(c + 1) * 128, D:3 * D],
                            serialize=False)

        kvbanks = [pz, pw, pk[0], pk[1]]
        kt_free = [None, None]
        vt_free = [None, None]
        g_kvmm = None
        for i in range(16):
            par = i % 2
            ktb = A[:, A_KT + par * 1024: A_KT + (par + 1) * 1024]
            vtb = A[:, A_VT + par * 1040: A_VT + (par + 1) * 1040]
            banks = [psum.take("tensor") for _ in range(4)]
            rec.wait("tensor", g_wkv)
            rec.wait("tensor", g_ln2)
            rec.wait("tensor", g_bkv)
            for c in range(NCD):
                lnslice = A[:, c * 2048 + i * 128: c * 2048 + (i + 1) * 128]
                for ft in range(4):
                    rec.op("tensor", "matmul", banks[ft]["t"][:, :],
                           lnslice, wkv_ap[c][:, ft * 512:(ft + 1) * 512],
                           start=(c == 0), stop=False)
            g_prod = None
            for ft in range(4):
                g_prod = rec.op("tensor", "matmul", banks[ft]["t"][:, :],
                                ones_r[:, :], bkv_bf[0:1, ft * 512:(ft + 1) * 512],
                                start=False, stop=True)

            rec.wait("scalar", g_prod)
            rec.wait("scalar", kt_free[par])
            g_k = None
            for ft in range(2):
                _, g_k = elu_evac(banks[ft]["t"][:, :],
                                  ktb[:, ft * 512:(ft + 1) * 512], 0.0, 0.0)
                banks[ft]["free"] = g_k
            rec.wait("vector", g_prod)
            rec.wait("vector", vt_free[par])
            rec.wait("vector", g_ones)
            g_v = None
            for ft in range(2, 4):
                src = banks[ft]["t"][:, :].rearrange("p (h e) -> p h e", e=64)
                dstv = vtb.rearrange("p (h x) -> p h x", x=65)[
                    :, (ft - 2) * 8:(ft - 1) * 8, 0:64]
                g_v = rec.op("vector", "tensor_copy", out=dstv, in_=src)
                banks[ft]["free"] = g_v

            rec.wait("tensor", g_k)
            rec.wait("tensor", g_v)
            if i == 0:
                rec.waits("tensor", lnst["pzfree"])
                rec.wait("tensor", g_a2)
            for h in range(16):
                bank = kvbanks[(h % 2) * 2 + h // 8]
                s = (h % 8) // 2
                out_ap = bank[0:64, s * 65:(s + 1) * 65]
                g_kvmm = rec.op("tensor", "matmul", out_ap,
                                ktb[:, h * 64:(h + 1) * 64],
                                vtb[:, h * 65:(h + 1) * 65],
                                start=(i == 0 and s == 0),
                                stop=(i == 15 and s == 3))
            kt_free[par] = g_kvmm
            vt_free[par] = g_kvmm

        rec.wait("vector", g_kvmm)
        rec.wait("scalar", g_kvmm)
        rec.op("vector", "tensor_copy", out=kv_sb[0:64, 0:260],
               in_=kvbanks[0][0:64, 0:260])
        g_ev_v = rec.op("vector", "tensor_copy", out=kv_sb[0:64, 260:520],
                        in_=kvbanks[1][0:64, 0:260])
        rec.op("scalar", "copy", kv_sb[64:128, 0:260], kvbanks[2][0:64, 0:260])
        g_ev_s = rec.op("scalar", "copy", kv_sb[64:128, 260:520],
                        kvbanks[3][0:64, 0:260])
        rec.wait("gpsimd", g_ev_v)
        rec.wait("gpsimd", g_ev_s)
        g_ccin = rec.dma("gpsimd", "ccb", out=cc_in[:, :], in_=kv_sb[:, :])
        lnst["pzfree"] = [g_ccin]

        def wq_ap(c):
            if c < 6:
                return C[:, c * 1024:(c + 1) * 1024]
            return lnA[:, (c - 6) * 1024:(c - 5) * 1024]

        rec.wait("gpsimd", g_kvmm)
        g_wq = None
        for c in range(NCD):
            g_wq = rec.dma("gpsimd", "wqc", out=wq_ap(c),
                           in_=wqkv_in[c * 128:(c + 1) * 128, 0:D],
                           serialize=False)

        rec.wait("gpsimd", g_ccin)
        cc_sem.n += 1

        def cc_fn(e, v=cc_sem.n, sh=cc_sem.h, groups=groups):
            e.collective_compute(
                "AllReduce", ALU.add, replica_groups=groups,
                ins=[cc_in.ap()], outs=[cc_out.ap()]).then_inc(sh, 1)
            e.wait_ge(sh, v)

        rec.ops["gpsimd"].append(cc_fn)
        g_kvback = rec.dma("gpsimd", "ccb", out=kv_sb[:, :], in_=cc_out[:, :])

        if debug:
            rec.wait("sync", g_kvback)
            dg = rec.dma("sync", "dbg", out=dbg_kv[:, :], in_=kv_sb[:, :])
            rec.wait("vector", dg)

        rec.wait("vector", g_kvback)
        g_kvbf = rec.op("vector", "tensor_copy", out=kv_bf[:, :], in_=kv_sb[:, :])
        rec.wait("vector", g_kvbf)
        rec.wait("vector", g_ksq_ms)
        g_ksq = None
        for h in range(16):
            half, idx = h // 8, h % 8
            cc2, hpar = idx // 2, idx % 2
            col = half * 260 + cc2 * 65 + 64
            pr = slice(hpar * 64, (hpar + 1) * 64)
            dcol = (h // 2) * 16 + h
            g_ksq = rec.op("vector", "tensor_copy", out=KSQ[pr, dcol:dcol + 1],
                           in_=kv_bf[pr, col:col + 1])

        QBASE = {0: A_QB, 1: 16384}
        pzq_free = {0: None, 1: None}
        q_done = {}
        qmm_last = [None]
        attn_g = {}

        def q_block(i):
            base = QBASE[i % 2]
            g_q = None
            for qc in range(NCD):
                bq = psum.take("tensor")
                rec.wait("tensor", g_wq)
                g_mm = None
                for c in range(NCD):
                    g_mm = rec.op(
                        "tensor", "matmul", bq["t"][:, :],
                        wq_ap(c)[:, qc * 128:(qc + 1) * 128],
                        A[:, c * 2048 + i * 512: c * 2048 + (i + 1) * 512],
                        start=(c == 0), stop=(c == NCD - 1))
                rec.wait("scalar", g_mm)
                rec.wait("scalar", g_cst)
                rec.wait("scalar", g_const_v)
                if qc == 0:
                    rec.wait("scalar", pzq_free[i % 2])
                _, g_q = elu_evac(bq["t"][:, :],
                                  A[:, base + qc * 512: base + (qc + 1) * 512],
                                  bqc[:, qc:qc + 1], nbqc[:, qc:qc + 1])
                bq["free"] = g_q
            q_done[i] = g_q
            qmm_last[0] = (rec.esem["tensor"], rec.esem["tensor"].n)

        def zout_block(i):
            base = QBASE[i % 2]
            rec.wait("tensor", q_done[i])
            rec.wait("tensor", g_ksq)
            rec.waits("tensor", lnst["pzfree"])
            g_z = None
            for qc in range(NCD):
                g_z = rec.op("tensor", "matmul", pz[0:16, :],
                             KSQ[:, qc * 16:(qc + 1) * 16],
                             A[:, base + qc * 512: base + (qc + 1) * 512],
                             start=(qc == 0), stop=(qc == NCD - 1))
            rec.wait("vector", g_z)
            g_ze = rec.op("vector", "tensor_scalar", out=pz[0:16, :],
                          in0=pz[0:16, :], scalar1=EPS_Z, scalar2=None,
                          op0=ALU.add)
            rec.wait("vector", g_ze)
            g_zr = rec.op("vector", "reciprocal", pz[0:16, :], pz[0:16, :])
            rec.wait("vector", g_zr)
            g_zbf = rec.op("vector", "tensor_copy", out=zbf[:, :],
                           in_=pz[0:16, :])
            lnst["pzfree"] = [g_zbf, g_ccin]

            g_out = None
            for qc in range(NCD):
                bz = psum.take("tensor")
                bo = psum.take("tensor")
                rec.wait("tensor", g_zbf)
                rec.wait("tensor", g_ind)
                rec.op("tensor", "matmul", bz["t"][:, :],
                       IND[:, qc * 128:(qc + 1) * 128], zbf[:, :],
                       start=True, stop=True)
                h0, h1 = 2 * qc, 2 * qc + 1
                for hh, pr in ((h0, slice(0, 64)), (h1, slice(64, 128))):
                    half, idx = hh // 8, hh % 8
                    cc2 = idx // 2
                    col = half * 260 + cc2 * 65
                    rec.wait("tensor", g_kvbf)
                    g_mm = rec.op("tensor", "matmul", bo["t"][pr, :],
                                  kv_bf[pr, col:col + 64],
                                  A[pr, base + qc * 512: base + (qc + 1) * 512],
                                  start=True, stop=True)
                tt = tpool.take("scalar")
                rec.wait("scalar", g_mm)
                g_zc = rec.op("scalar", "copy", tt["t"][:, :], bz["t"][:, :])
                bz["free"] = g_zc
                rec.wait("vector", g_zc)
                g_out = rec.op("vector", "tensor_tensor",
                               out=A[:, qc * 2048 + i * 512:
                                     qc * 2048 + (i + 1) * 512],
                               in0=bo["t"][:, :], in1=tt["t"][:, :], op=ALU.mult)
                bo["free"] = g_out
                tt["free"] = g_out
                attn_g[(qc, i)] = g_out
            pzq_free[i % 2] = g_out

        q_block(0)
        q_block(1)
        for i in range(NSUB):
            zout_block(i)
            if i + 2 < NSUB:
                q_block(i + 2)

        last_q_mm = qmm_last[0]
        g_wp = None
        for c in range(NCD):
            rec.wait("gpsimd", last_q_mm)
            g_wp = rec.dma("gpsimd", "wpjc", out=wq_ap(c),
                           in_=wproj_in[c * 128:(c + 1) * 128, :],
                           serialize=False)

        for tau in range(2):
            for g in range(NCD):
                banks = [psum.take("tensor"), psum.take("tensor")]
                rec.wait("tensor", g_wp)
                for c in range(NCD):
                    for s in range(2):
                        i = tau * 2 + s
                        rec.wait("tensor", attn_g[(c, i)])
                        g_mm = rec.op(
                            "tensor", "matmul", banks[s]["t"][:, :],
                            wq_ap(c)[:, g * 128:(g + 1) * 128],
                            A[:, c * 2048 + i * 512: c * 2048 + (i + 1) * 512],
                            start=(c == 0), stop=(c == NCD - 1))
                rec.wait("scalar", g_mm)
                rec.wait("scalar", g_cst)
                for s in range(2):
                    i = tau * 2 + s
                    tt = tpool.take("scalar")
                    g_ev = rec.op("scalar", "activation", tt["t"][:, :],
                                  banks[s]["t"][:, :], AF.Identity,
                                  bias=bprojc[:, g:g + 1], scale=1.0)
                    banks[s]["free"] = g_ev
                    xt = x8[:, g * T + i * 512: g * T + (i + 1) * 512]
                    rec.wait("vector", g_ev)
                    x8g = rec.op("vector", "tensor_tensor", out=xt, in0=xt,
                                 in1=tt["t"][:, :], op=ALU.add)
                    tt["free"] = x8g
                    x_g[(g, i)] = x8g

        if debug:
            for g in range(NCD):
                for i in range(NSUB):
                    rec.wait("sync", x_g[(g, i)])
                dg = rec.dma("sync", "dbg", out=dbg_r2[g * 128:(g + 1) * 128, :],
                             in_=x8[:, g * T:(g + 1) * T])
            rec.wait("vector", dg)

        proj_done = (rec.esem["tensor"], rec.esem["tensor"].n)
        h_entry_gates.append(proj_done)
        wslot1[0] = proj_done
        wslot1[1] = proj_done
        wslot2[0] = proj_done
        wslot2[1] = proj_done

        g_lns = ln_pass(lambda c, i: lnA[:, c * 2048 + i * 512:
                                         c * 2048 + (i + 1) * 512])
        for tau in range(2):
            ffn_tau(tau, g_lns[tau * 2 + 1])

        for g in range(NCD):
            for i in range(NSUB):
                rec.wait("sync", x_g[(g, i)])
            rec.dma("sync", "yst", out=y_out[g * 128:(g + 1) * 128, :],
                    in_=x8[:, g * T:(g + 1) * T], serialize=False)
        fin = (rec.dsem["yst"], rec.dsem["yst"].n)

        def fin_fn(e, sh=fin[0].h, v=fin[1]):
            e.wait_ge(sh, v)

        rec.ops["sync"].append(fin_fn)
        rec.ops["gpsimd"].append(fin_fn)

        with nc.Block() as block:
            @block.tensor
            def _(e):
                for fn in rec.ops["tensor"]:
                    fn(e)

            @block.vector
            def _(e):
                for fn in rec.ops["vector"]:
                    fn(e)

            @block.scalar
            def _(e):
                for fn in rec.ops["scalar"]:
                    fn(e)

            @block.gpsimd
            def _(e):
                for fn in rec.ops["gpsimd"]:
                    fn(e)

            @block.sync
            def _(e):
                for fn in rec.ops["sync"]:
                    fn(e)

    return nc


def kernel(L0_attn_x, L0_attn_w_qkv, L0_attn_b_qkv, L0_attn_w_proj, L0_attn_b_proj,
           L0_ffn_w1, L0_ffn_b1, L0_ffn_w2, L0_ffn_b2):
    x = np.asarray(L0_attn_x, dtype=np.float32)
    wqkv = np.ascontiguousarray(np.asarray(L0_attn_w_qkv, dtype=np.float32))
    bqkv = np.asarray(L0_attn_b_qkv, dtype=np.float32)
    wproj = np.ascontiguousarray(np.asarray(L0_attn_w_proj, dtype=np.float32))
    bproj = np.asarray(L0_attn_b_proj, dtype=np.float32)
    w1 = np.asarray(L0_ffn_w1, dtype=np.float32)
    b1 = np.asarray(L0_ffn_b1, dtype=np.float32)
    w2 = np.asarray(L0_ffn_w2, dtype=np.float32)
    b2 = np.asarray(L0_ffn_b2, dtype=np.float32)

    w1r = np.ascontiguousarray(
        w1.T.reshape(NCD, 128, NCH, 128).transpose(2, 1, 0, 3).reshape(NCH, 128, 1024))
    w2r = np.ascontiguousarray(
        w2.T.reshape(NCH, 128, NCD, 128).transpose(2, 1, 0, 3).reshape(NCD, 128, 4096))
    b1c = np.ascontiguousarray(b1.reshape(NCH, 128).T)
    b2c = np.ascontiguousarray(b2.reshape(NCD, 128).T)
    bqc = np.ascontiguousarray(bqkv[0:D].reshape(NCD, 128).T)
    bkv = np.ascontiguousarray(bqkv[D:3 * D].reshape(1, 2 * D))
    bprojc = np.ascontiguousarray(bproj.reshape(NCD, 128).T)

    import ml_dtypes
    ind = np.zeros((16, 1024), dtype=ml_dtypes.bfloat16)
    for c in range(NCD):
        for par in range(2):
            ind[2 * c + par, c * 128 + par * 64: c * 128 + par * 64 + 64] = 1.0

    nc = build_nc([[0, 1], [2, 3], [4, 5], [6, 7]])
    in_maps = []
    for core in range(8):
        b, s = core // 2, core % 2
        xs = np.ascontiguousarray(x[b, s * T:(s + 1) * T, :].T)
        in_maps.append({
            "x": xs, "w1r": w1r, "w2r": w2r, "wqkv": wqkv, "wproj": wproj,
            "bkv": bkv, "bqc": bqc, "bprojc": bprojc, "b1c": b1c, "b2c": b2c,
            "ind": ind,
        })
    res = run_bass_kernel_spmd(nc, in_maps, core_ids=list(range(8)))
    out = np.empty((4, 4096, D), dtype=np.float32)
    for core in range(8):
        b, s = core // 2, core % 2
        out[b, s * T:(s + 1) * T, :] = np.asarray(res.results[core]["y"]).T
    return out
